# version 04 (final)
# baseline (speedup 1.0000x reference)
import math
from contextlib import ExitStack

import numpy as np
import ml_dtypes

import concourse.bass as bass
import concourse.mybir as mybir
from concourse.bass_utils import run_bass_kernel_spmd

F32 = mybir.dt.float32
BF16 = mybir.dt.bfloat16
I32 = mybir.dt.int32
AF = mybir.ActivationFunctionType
ALU = mybir.AluOpType

NF4 = np.array([
    -1.0, -0.6961928009986877, -0.5250730514526367, -0.39491748809814453,
    -0.28444138169288635, -0.18477343022823334, -0.09105003625154495, 0.0,
    0.07958029955625534, 0.16093020141124725, 0.2461123019456863,
    0.33791524171829224, 0.44070982933044434, 0.5626170039176941,
    0.7229568362236023, 1.0], dtype=np.float64)
DELTA = [float(NF4[0])] + [float(NF4[t] - NF4[t - 1]) for t in range(1, 16)]

ATT_SCALE = 1.0 / math.sqrt(128.0)
EXP_BIAS = -12.0
EPS = 1e-5


def full_cfg():
    return dict(T=4096, D=4096, F=11008)


def small_cfg():
    return dict(T=1024, D=1024, F=2048)


def build_nc(cfg, debug_names=()):
    T, D, F = cfg["T"], cfg["D"], cfg["F"]
    R = 16
    HD = 128
    H = D // HD
    G = H // 4
    KVD = 2 * HD * G
    OWN = T // 2
    NQT = OWN // 128
    NG = OWN // 512
    DC = D // 128
    FC = F // 128
    NKT = T // 128
    SH = D // 8
    SHC = SH // 128
    NB_GU = SH // 64
    DCH = []
    off = 0
    while off < F:
        w = min(512, F - off)
        DCH.append((off, w))
        off += w

    nc = bass.Bass("TRN2", target_bir_lowering=False, debug=False,
                   detect_race_conditions=False)

    NB = 4
    GL = 2
    KVL = 2 * HD * GL
    NJOB = NB * NKT
    inp = {}
    for nm, shape, dt in [
            ("xkv", [NB * T, D], F32), ("xq", [OWN, D], F32),
            ("wqr", [D, D], F32), ("wkv", [D, KVL], F32), ("wor", [D, D], F32),
            ("bo", [D, 1], F32), ("maskin", [1024, 512], BF16),
            ("gcod", [F, SH], I32), ("gam", [F, NB_GU], F32),
            ("ucod", [F, SH], I32), ("uam", [F, NB_GU], F32),
            ("dcod", [SH, F], I32), ("dam", [SH, F // 64], F32),
            ("gla", [D, R], F32), ("glb", [R, F], F32),
            ("ula", [D, R], F32), ("ulb", [R, F], F32),
            ("dla", [F, R], F32), ("dlb", [R, D], F32)]:
        inp[nm] = nc.dram_tensor(nm, shape, dt, kind="ExternalInput").ap()
    out = nc.dram_tensor("out", [OWN, D], F32, kind="ExternalOutput").ap()

    dram = {}
    for nm, shape, dt in [
            ("wq_bf", [D, D], BF16), ("wkv_bf", [D, KVL], BF16), ("wo_bf", [D, D], BF16),
            ("glb_bf", [R, F], BF16), ("ulb_bf", [R, F], BF16), ("dlb_bf", [R, D], BF16),
            ("gnat", [F, SH], BF16), ("unat", [F, SH], BF16), ("dnat", [SH, F], BF16),
            ("gsrc_g", [SH, F], BF16), ("gsrc_u", [SH, F], BF16), ("gsrc_d", [F, SH], BF16),
            ("gath_g", [D, F], BF16), ("gath_u", [D, F], BF16), ("gath_d", [8 * F, SH], BF16),
            ("qT", [D, OWN], BF16), ("ctxT", [D, OWN], BF16),
            ("aon", [OWN, D], F32), ("x2f", [OWN, D], F32), ("ln2T", [D, OWN], BF16),
            ("x3T", [F, OWN], BF16)]:
        dram[nm] = nc.dram_tensor(nm, shape, dt, kind="Internal").ap()

    dbg_out = {}
    for nm in debug_names:
        src = dram[nm]
        dbg_out[nm] = nc.dram_tensor("dbg_" + nm, list(src.shape), src.dtype,
                                     kind="ExternalOutput").ap()

    es = ExitStack()

    class Ctr:
        def __init__(self, name):
            self.h = es.enter_context(nc.semaphore(name))
            self.v = 0

        def inc(self, instr, n=1):
            instr.then_inc(self.h, n)
            self.v += n
            return self.v

    identB = nc.alloc_sbuf_tensor("identB", [128, 128], BF16)
    identF = nc.alloc_sbuf_tensor("identF", [128, 128], F32)
    iai = nc.alloc_sbuf_tensor("iai", [128, 128], F32)
    boS = nc.alloc_sbuf_tensor("boS", [128, DC], F32)
    tp_ld = [nc.alloc_sbuf_tensor(f"tp_ld{i}", [128, 512], BF16) for i in range(2)]
    tp_blk = [nc.alloc_sbuf_tensor(f"tp_blk{i}", [128, 128], BF16) for i in range(2)]
    ln_ssq = nc.alloc_sbuf_tensor("ln_ssq", [128, 1], F32)
    ln_ssm = nc.alloc_sbuf_tensor("ln_ssm", [128, 1], F32)
    ln_mn = nc.alloc_sbuf_tensor("ln_mn", [128, 1], F32)
    ln_m2 = nc.alloc_sbuf_tensor("ln_m2", [128, 1], F32)
    ln_vr = nc.alloc_sbuf_tensor("ln_vr", [128, 1], F32)
    ln_rs = nc.alloc_sbuf_tensor("ln_rs", [128, 1], F32)
    ln_bv = nc.alloc_sbuf_tensor("ln_bv", [128, 1], F32)
    ln_t1 = nc.alloc_sbuf_tensor("ln_t1", [128, 1], F32)
    ln_dump = nc.alloc_sbuf_tensor("ln_dump", [128, D], BF16)
    c_eps = nc.alloc_sbuf_tensor("c_eps", [128, 1], F32)
    c_nb = nc.alloc_sbuf_tensor("c_nb", [128, 1], F32)

    s_ph = Ctr("s_ph")
    CUM = [0]

    def ph_wait(eng):
        eng.wait_ge(s_ph.h, CUM[-1])

    def ph_end(n):
        CUM.append(CUM[-1] + n)

    s_ag = Ctr("s_ag")

    dq_es = ExitStack()
    dq_ct = [dq_es.enter_context(nc.sbuf_tensor(f"dq_ct{i}", [128, 512], I32))
             for i in range(2)]
    dq_am = [dq_es.enter_context(nc.sbuf_tensor(f"dq_am{i}", [128, 16], F32))
             for i in range(2)]
    dq_cf = dq_es.enter_context(nc.sbuf_tensor("dq_cf", [128, 512], F32))
    dq_ac = [dq_es.enter_context(nc.sbuf_tensor(f"dq_ac{i}", [128, 512], F32))
             for i in range(2)]
    dq_tm = dq_es.enter_context(nc.sbuf_tensor("dq_tm", [128, 512], F32))
    dq_wn = [dq_es.enter_context(nc.sbuf_tensor(f"dq_wn{i}", [128, 512], BF16))
             for i in range(2)]

    with nc.Block() as blk:
        s_init = Ctr("s_init")

        def f_gp(gp):
            gp.iota(iai[:, :], [[-1, 128]], base=0, channel_multiplier=1,
                    allow_small_or_imprecise_dtypes=True).then_inc(s_init.h, 1)
            gp.memset(c_eps[:, :], EPS)
            gp.memset(c_nb[:, :], EXP_BIAS)
            gp.sem_inc(s_ph.h, 1)
        blk.gpsimd(f_gp)

        def f_ve(ve):
            ve.wait_ge(s_init.h, 1)
            ve.tensor_single_scalar(identB[:], iai[:], 0.0, ALU.is_equal)
            ve.tensor_single_scalar(identF[:], iai[:], 0.0, ALU.is_equal).then_inc(s_ph.h, 1)
        blk.vector(f_ve)

        s_ini2 = Ctr("s_ini2")

        def f_sc(sc):
            sc.dma_start(out=boS[:, :],
                         in_=inp["bo"].rearrange("(c p) o -> p c o", p=128),
                         allow_slow_non_contiguous=True).then_inc(s_ini2.h, 16)
            sc.wait_ge(s_ini2.h, 16)
            sc.sem_inc(s_ph.h, 1)
        blk.scalar(f_sc)
    ph_end(3)

    s_dq_ld = Ctr("s_dq_ld")
    s_dq_st = Ctr("s_dq_st")
    dq_jobs = []
    for cod, amt, nat in ((inp["gcod"], inp["gam"], dram["gnat"]),
                          (inp["ucod"], inp["uam"], dram["unat"])):
        for r in range(FC):
            dq_jobs.append((nat[r * 128:(r + 1) * 128, :], cod[r * 128:(r + 1) * 128, :],
                            amt[r * 128:(r + 1) * 128, :], SH, NB_GU))
    for r in range(SHC):
        for (coff, cw) in DCH:
            dq_jobs.append((dram["dnat"][r * 128:(r + 1) * 128, coff:coff + cw],
                            inp["dcod"][r * 128:(r + 1) * 128, coff:coff + cw],
                            inp["dam"][r * 128:(r + 1) * 128, coff // 64:(coff + cw) // 64],
                            cw, cw // 64))
    N_DQ = len(dq_jobs)
    DQ_ST_G = 16 * FC
    DQ_ST_U = 16 * 2 * FC
    DQ_ST_D = 16 * N_DQ

    s_dq_v = Ctr("s_dq_v")
    s_dqc = [Ctr("s_dqc0"), Ctr("s_dqc1")]
    with nc.Block() as blk:
        def f_sy(sy):
            ph_wait(sy)
            for i, (dst, cap, aap, w, nb) in enumerate(dq_jobs):
                if i >= 2:
                    sy.wait_ge(s_dq_v.h, i - 1)
                sy.dma_start(out=dq_ct[i % 2][:, :w], in_=cap).then_inc(s_dqc[i % 2].h, 16)
                sy.dma_start(out=dq_am[i % 2][:, :nb], in_=aap).then_inc(s_dqc[i % 2].h, 16)
                if i >= 1:
                    sy.wait_ge(s_dq_v.h, i)
                    dstp, _, _, wp, _ = dq_jobs[i - 1]
                    sy.dma_start(out=dstp, in_=dq_wn[(i - 1) % 2][:, :wp]
                                 ).then_inc(s_dq_st.h, 16)
            sy.wait_ge(s_dq_v.h, N_DQ)
            dstp, _, _, wp, _ = dq_jobs[N_DQ - 1]
            sy.dma_start(out=dstp, in_=dq_wn[(N_DQ - 1) % 2][:, :wp]
                         ).then_inc(s_dq_st.h, 16)
            sy.wait_ge(s_dq_st.h, 16 * N_DQ)
            sy.sem_inc(s_ph.h, 1)
        blk.sync(f_sy)

        def f_ve(ve):
            ph_wait(ve)
            for i, (dst, cap, aap, w, nb) in enumerate(dq_jobs):
                b = i % 2
                ve.wait_ge(s_dqc[b].h, 32 * (i // 2 + 1))
                ve.tensor_copy(dq_cf[:, :w], dq_ct[b][:, :w])
                ve.tensor_scalar(dq_ac[0][:, :w], dq_cf[:, :w], -0.5, DELTA[0],
                                 ALU.is_ge, ALU.mult)
                cur = 0
                for t in range(1, 16):
                    ve.tensor_scalar(dq_tm[:, :w], dq_cf[:, :w], t - 0.5, DELTA[t],
                                     ALU.is_ge, ALU.mult)
                    ve.tensor_tensor(dq_ac[1 - cur][:, :w], dq_ac[cur][:, :w],
                                     dq_tm[:, :w], ALU.add)
                    cur = 1 - cur
                if i >= 2:
                    ve.wait_ge(s_dq_st.h, 16 * (i - 1))
                last = None
                for bl in range(nb):
                    last = ve.tensor_scalar_mul(dq_wn[b][:, bl * 64:(bl + 1) * 64],
                                                dq_ac[cur][:, bl * 64:(bl + 1) * 64],
                                                dq_am[b][:, bl:bl + 1])
                s_dq_v.inc(last)
            ve.sem_inc(s_ph.h, 1)
        blk.vector(f_ve)
    ph_end(2)

    cast_es = ExitStack()
    with nc.Block() as blk:
        s_cast = Ctr("s_cast")
        castF = [cast_es.enter_context(nc.sbuf_tensor(f"castF{i}", [128, D], F32))
                 for i in range(2)]
        castB = [cast_es.enter_context(nc.sbuf_tensor(f"castB{i}", [128, D], BF16))
                 for i in range(2)]

        def f_sc(sc):
            ph_wait(sc)
            jobs = []
            for r in range(DC):
                jobs.append((inp["wqr"][r * 128:(r + 1) * 128, :],
                             dram["wq_bf"][r * 128:(r + 1) * 128, :], 128, D))
            for r in range(DC):
                jobs.append((inp["wkv"][r * 128:(r + 1) * 128, :],
                             dram["wkv_bf"][r * 128:(r + 1) * 128, :], 128, KVL))
            for r in range(DC):
                jobs.append((inp["wor"][r * 128:(r + 1) * 128, :],
                             dram["wo_bf"][r * 128:(r + 1) * 128, :], 128, D))
            for src, dst, width in (("glb", "glb_bf", F), ("ulb", "ulb_bf", F),
                                    ("dlb", "dlb_bf", D)):
                for c0 in range(0, width, D):
                    cw = min(D, width - c0)
                    jobs.append((inp[src][:, c0:c0 + cw],
                                 dram[dst][:, c0:c0 + cw], R, cw))
            s_cc = Ctr("s_cc")
            for i, (src, dst, rows, w) in enumerate(jobs):
                b = i % 2
                s_cast.inc(sc.dma_start(out=castF[b][0:rows, :w], in_=src), 16)
                sc.wait_ge(s_cast.h, s_cast.v)
                s_cc.inc(sc.copy(castB[b][0:rows, :w], castF[b][0:rows, :w]))
                sc.wait_ge(s_cc.h, s_cc.v)
                s_cast.inc(sc.dma_start(out=dst, in_=castB[b][0:rows, :w]), 16)
            sc.wait_ge(s_cast.h, s_cast.v)
            sc.sem_inc(s_ph.h, 1)
        blk.scalar(f_sc)
    ph_end(1)
    cast_es.close()

    def emit_ln_scalar(sc, s_sc, s_gp, xt, w, xln, gp_base):
        sc.activation(ln_dump[:, :w], xt[:, :w], AF.Square, accum_out=ln_ssq[:, :])
        s_sc.inc(sc.activation(ln_dump[:, :w], xt[:, :w], AF.Identity,
                               accum_out=ln_ssm[:, :]))
        sc.wait_ge(s_gp.h, gp_base + 1)
        sc.activation(ln_rs[:, :], ln_vr[:, :], AF.Ln, bias=c_eps[:, :])
        s_sc.inc(sc.activation(ln_rs[:, :], ln_rs[:, :], AF.Exp, scale=-0.5))
        sc.wait_ge(s_gp.h, gp_base + 2)
        s_sc.inc(sc.activation(xln[:, :w], xt[:, :w], AF.Identity,
                               bias=ln_bv[:, :], scale=ln_rs[:, :]))

    def emit_ln_gp(gp, s_sc, s_gp, sc_base, w):
        inv = 1.0 / w
        gp.wait_ge(s_sc.h, sc_base + 1)
        gp.tensor_scalar_mul(ln_mn[:, :], ln_ssm[:, :], inv)
        gp.tensor_mul(ln_m2[:, :], ln_mn[:, :], ln_mn[:, :])
        gp.tensor_scalar_mul(ln_t1[:, :], ln_ssq[:, :], inv)
        s_gp.inc(gp.tensor_sub(ln_vr[:, :], ln_t1[:, :], ln_m2[:, :]))
        gp.wait_ge(s_sc.h, sc_base + 2)
        gp.tensor_scalar_mul(ln_t1[:, :], ln_mn[:, :], -1.0)
        s_gp.inc(gp.tensor_mul(ln_bv[:, :], ln_t1[:, :], ln_rs[:, :]))

    p1_big = ExitStack()
    xqlnT = p1_big.enter_context(nc.sbuf_tensor("xqlnT", [128, DC * OWN], BF16))
    with nc.Block() as blk:
        s1_ld = Ctr("s1_ld")
        s1_sc = Ctr("s1_sc")
        s1_gp = Ctr("s1_gp")
        s1_tr = Ctr("s1_tr")
        s1_ev = Ctr("s1_ev")
        p1es = ExitStack()
        p1_xt = p1es.enter_context(nc.sbuf_tensor("p1_xt", [128, D], F32))
        p1_xln = [p1es.enter_context(nc.sbuf_tensor(f"p1_xln{i}", [128, D], BF16))
                  for i in range(2)]
        psB = [p1es.enter_context(nc.psum_tensor(f"p1_psB{i}", [128, DC * 128], BF16))
               for i in range(2)]

        def f_sy(sy):
            ph_wait(sy)
            for i in range(NQT):
                if i >= 1:
                    sy.wait_ge(s1_sc.h, 3 * i)
                sy.dma_start(out=p1_xt[:, :],
                             in_=inp["xq"][i * 128:(i + 1) * 128, :]).then_inc(s1_ld.h, 16)
            sy.wait_ge(s1_ev.h, NQT)
            sy.sem_inc(s_ph.h, 1)
        blk.sync(f_sy)

        def f_sc(sc):
            ph_wait(sc)
            for i in range(NQT):
                sc.wait_ge(s1_ld.h, 16 * (i + 1))
                if i >= 2:
                    sc.wait_ge(s1_tr.h, i - 1)
                emit_ln_scalar(sc, s1_sc, s1_gp, p1_xt, D, p1_xln[i % 2],
                               gp_base=2 * i)
                sc.wait_ge(s1_tr.h, i + 1)
                s1_ev.inc(sc.copy(
                    xqlnT[:].rearrange("p (c t) -> p c t", c=DC)[:, :, i * 128:(i + 1) * 128],
                    psB[i % 2][:].rearrange("p (c q) -> p c q", c=DC)))
            sc.sem_inc(s_ph.h, 1)
        blk.scalar(f_sc)

        def f_gp(gp):
            ph_wait(gp)
            for i in range(NQT):
                emit_ln_gp(gp, s1_sc, s1_gp, sc_base=3 * i, w=D)
            gp.sem_inc(s_ph.h, 1)
        blk.gpsimd(f_gp)

        def f_te(te):
            ph_wait(te)
            for i in range(NQT):
                te.wait_ge(s1_sc.h, 3 * i + 3)
                if i >= 2:
                    te.wait_ge(s1_ev.h, i - 1)
                tr = None
                for c in range(DC):
                    tr = te.transpose(psB[i % 2][:, c * 128:(c + 1) * 128],
                                      p1_xln[i % 2][:, c * 128:(c + 1) * 128], identB[:])
                s1_tr.inc(tr)
            te.sem_inc(s_ph.h, 1)
        blk.tensor(f_te)
        p1es.close()
    ph_end(4)

    with nc.Block() as blk:
        s2b_w = [Ctr("s2b_w0"), Ctr("s2b_w1")]
        s2b_mm = Ctr("s2b_mm")
        s2b_ev = Ctr("s2b_ev")
        s2b_st = Ctr("s2b_st")
        bes = ExitStack()
        wqp = [bes.enter_context(nc.sbuf_tensor(f"wqp{i}", [128, DC * 128], BF16))
               for i in range(2)]
        qev = [bes.enter_context(nc.sbuf_tensor(f"qev{i}", [128, OWN], BF16))
               for i in range(2)]
        psQ = [bes.enter_context(nc.psum_tensor(f"psQ{i}", [128, 512], F32))
               for i in range(2 * NG)]

        def f_sy(sy):
            ph_wait(sy)
            for qd in range(DC):
                if qd >= 2:
                    sy.wait_ge(s2b_mm.h, qd - 1)
                sy.dma_start(out=wqp[qd % 2][:, :],
                             in_=dram["wq_bf"][qd * 128:(qd + 1) * 128, :]
                             ).then_inc(s2b_w[qd % 2].h, 16)
            sy.wait_ge(s2b_st.h, 16 * DC)
            sy.sem_inc(s_ph.h, 1)
        blk.sync(f_sy)

        def f_te(te):
            ph_wait(te)
            for qd in range(DC):
                te.wait_ge(s2b_w[qd % 2].h, 16 * (qd // 2 + 1))
                if qd >= 2:
                    te.wait_ge(s2b_ev.h, qd - 1)
                ps_set = [psQ[(qd % 2) * NG + g] for g in range(NG)]
                mm = None
                for dm in range(DC):
                    for g in range(NG):
                        mm = te.matmul(ps_set[g][:, :],
                                       wqp[qd % 2][:, dm * 128:(dm + 1) * 128],
                                       xqlnT[:, dm * OWN + g * 512: dm * OWN + (g + 1) * 512],
                                       start=(dm == 0), stop=(dm == DC - 1))
                s2b_mm.inc(mm)
            te.sem_inc(s_ph.h, 1)
        blk.tensor(f_te)

        def f_sc(sc):
            ph_wait(sc)
            for qd in range(DC):
                sc.wait_ge(s2b_mm.h, qd + 1)
                if qd >= 2:
                    sc.wait_ge(s2b_st.h, 16 * (qd - 1))
                last = None
                for g in range(NG):
                    last = sc.copy(qev[qd % 2][:, g * 512:(g + 1) * 512],
                                   psQ[(qd % 2) * NG + g][:, :])
                s2b_ev.inc(last)
            sc.sem_inc(s_ph.h, 1)
        blk.scalar(f_sc)

        def f_gp(gp):
            ph_wait(gp)
            for qd in range(DC):
                gp.wait_ge(s2b_ev.h, qd + 1)
                gp.dma_start(out=dram["qT"][qd * 128:(qd + 1) * 128, :],
                             in_=qev[qd % 2][:, :]).then_inc(s2b_st.h, 16)
            gp.wait_ge(s2b_st.h, 16 * DC)
            gp.sem_inc(s_ph.h, 1)
        blk.gpsimd(f_gp)
        bes.close()
    ph_end(4)
    p1_big.close()

    def emit_tp(tag, wait_store):
        nat = dram[{"g": "gnat", "u": "unat", "d": "dnat"}[tag]]
        gsrc = dram["gsrc_" + tag]
        gout = dram["gath_" + tag]
        with nc.Block() as blk:
            st_ld = [Ctr(f"tp{tag}_ld0"), Ctr(f"tp{tag}_ld1")]
            st_tr = Ctr(f"tp{tag}_tr")
            st_ev = Ctr(f"tp{tag}_ev")
            st_st = Ctr(f"tp{tag}_st")
            tes = ExitStack()
            tps = [tes.enter_context(nc.psum_tensor(f"tp{tag}_ps{i}", [128, 1024], BF16))
                   for i in range(2)]

            if tag in ("g", "u"):
                strip = [tes.enter_context(
                    nc.sbuf_tensor(f"tp{tag}_strip{i}", [128, 1024], BF16))
                    for i in range(SHC)]
                tiles = [(nat[r * 128:(r + 1) * 128, :], SH, r) for r in range(FC)]

                def f_sy(sy):
                    ph_wait(sy)
                    sy.wait_ge(s_dq_st.h, wait_store)
                    for i, (src, w, r) in enumerate(tiles):
                        if i >= 2:
                            sy.wait_ge(st_tr.h, (i - 1) * SHC)
                        sy.dma_start(out=tp_ld[i % 2][:, :w], in_=src
                                     ).then_inc(st_ld[i % 2].h, 16)
                    sy.sem_inc(s_ph.h, 1)
                blk.sync(f_sy)

                def f_te(te):
                    ph_wait(te)
                    nt = 0
                    for i, (src, w, r) in enumerate(tiles):
                        te.wait_ge(st_ld[i % 2].h, 16 * (i // 2 + 1))
                        for bl in range(SHC):
                            if nt >= 2:
                                te.wait_ge(st_ev.h, nt - 1)
                            st_tr.inc(te.transpose(tps[nt % 2][:, 0:128],
                                                   tp_ld[i % 2][:, bl * 128:(bl + 1) * 128],
                                                   identB[:]))
                            nt += 1
                    te.sem_inc(s_ph.h, 1)
                blk.tensor(f_te)

                def f_sc(sc):
                    ph_wait(sc)
                    nt = 0
                    for i, (src, w, r) in enumerate(tiles):
                        for bl in range(SHC):
                            sc.wait_ge(st_tr.h, nt + 1)
                            if r % 8 == 0 and bl == 0 and r >= 8:
                                sc.wait_ge(st_st.h, 16 * SHC * (r // 8))
                            st_ev.inc(sc.copy(
                                strip[bl][:, (r % 8) * 128:(r % 8 + 1) * 128],
                                tps[nt % 2][:, 0:128]))
                            nt += 1
                    sc.sem_inc(s_ph.h, 1)
                blk.scalar(f_sc)

                def f_gp(gp):
                    ph_wait(gp)
                    nstr = 0
                    for r8 in range((FC + 7) // 8):
                        r_hi = min((r8 + 1) * 8, FC)
                        gp.wait_ge(st_ev.h, r_hi * SHC)
                        wcols = (r_hi - r8 * 8) * 128
                        for bl in range(SHC):
                            gp.dma_start(
                                out=gsrc[bl * 128:(bl + 1) * 128,
                                         r8 * 1024: r8 * 1024 + wcols],
                                in_=strip[bl][:, :wcols]).then_inc(st_st.h, 16)
                            nstr += 1
                    gp.wait_ge(st_st.h, 16 * nstr)
                    s_ag.inc(gp.collective_compute(
                        "AllGather", ALU.bypass, replica_groups=[list(range(8))],
                        ins=[gsrc[:, :]], outs=[gout[:, :]]))
                    gp.sem_inc(s_ph.h, 1)
                blk.gpsimd(f_gp)
            else:
                tiles = []
                for r in range(SHC):
                    for (coff, cw) in DCH:
                        tiles.append((nat[r * 128:(r + 1) * 128, coff:coff + cw], cw,
                                      (r, coff)))

                def f_sy(sy):
                    ph_wait(sy)
                    sy.wait_ge(s_dq_st.h, wait_store)
                    for i, (src, w, meta) in enumerate(tiles):
                        if i >= 2:
                            nb_done = sum(tiles[j][1] // 128 for j in range(i - 1))
                            sy.wait_ge(st_tr.h, nb_done)
                        sy.dma_start(out=tp_ld[i % 2][:, :w], in_=src
                                     ).then_inc(st_ld[i % 2].h, 16)
                    sy.sem_inc(s_ph.h, 1)
                blk.sync(f_sy)

                def f_te(te):
                    ph_wait(te)
                    nt = 0
                    for i, (src, w, meta) in enumerate(tiles):
                        te.wait_ge(st_ld[i % 2].h, 16 * (i // 2 + 1))
                        for bl in range(w // 128):
                            if nt >= 2:
                                te.wait_ge(st_ev.h, nt - 1)
                            st_tr.inc(te.transpose(tps[nt % 2][:, 0:128],
                                                   tp_ld[i % 2][:, bl * 128:(bl + 1) * 128],
                                                   identB[:]))
                            nt += 1
                    te.sem_inc(s_ph.h, 1)
                blk.tensor(f_te)

                def f_sc(sc):
                    ph_wait(sc)
                    nt = 0
                    for i, (src, w, (r, coff)) in enumerate(tiles):
                        for bl in range(w // 128):
                            sc.wait_ge(st_tr.h, nt + 1)
                            if nt >= 2:
                                sc.wait_ge(st_st.h, 16 * (nt - 1))
                            st_ev.inc(sc.copy(tp_blk[nt % 2][:, :], tps[nt % 2][:, 0:128]))
                            sc.wait_ge(st_ev.h, st_ev.v)
                            st_st.inc(sc.dma_start(
                                out=gsrc[coff + bl * 128:coff + (bl + 1) * 128,
                                         r * 128:(r + 1) * 128],
                                in_=tp_blk[nt % 2][:, :]), 16)
                            nt += 1
                    sc.sem_inc(s_ph.h, 1)
                blk.scalar(f_sc)

                def f_gp(gp):
                    ph_wait(gp)
                    gp.wait_ge(st_st.h, 16 * sum(w // 128 for _, w, _ in tiles))
                    s_ag.inc(gp.collective_compute(
                        "AllGather", ALU.bypass, replica_groups=[list(range(8))],
                        ins=[gsrc[:, :]], outs=[gout[:, :]]))
                    gp.sem_inc(s_ph.h, 1)
                blk.gpsimd(f_gp)
            tes.close()
        ph_end(4)

    emit_tp("g", DQ_ST_G)

    kv_es = ExitStack()
    KT = kv_es.enter_context(nc.sbuf_tensor("KT", [128, G * T], BF16))
    VC = kv_es.enter_context(nc.sbuf_tensor("VC", [128, NKT * G * 129], BF16))
    with nc.Block() as blk:
        sc_ld = Ctr("sc_ld")
        sc_w = [Ctr("sc_w0"), Ctr("sc_w1")]
        sc_sc = Ctr("sc_sc")
        sc_gp = Ctr("sc_gp")
        sc_mm = Ctr("sc_mm")
        sc_ev = Ctr("sc_ev")
        sc_ktr = Ctr("sc_ktr")
        sc_kev = Ctr("sc_kev")
        ces = ExitStack()
        c_xt = ces.enter_context(nc.sbuf_tensor("c_xt", [128, D], F32))
        c_xln = ces.enter_context(nc.sbuf_tensor("c_xln", [128, D], BF16))
        c_lnT = ces.enter_context(nc.sbuf_tensor("c_lnT", [128, DC * 128], BF16))
        c_wkv = [ces.enter_context(nc.sbuf_tensor(f"c_wkv{i}", [128, KVL], BF16))
                 for i in range(2)]
        c_kn = ces.enter_context(nc.sbuf_tensor("c_kn", [128, GL * 128], BF16))
        psT = ces.enter_context(nc.psum_tensor("c_psT", [128, DC * 128], BF16))
        psKV = ces.enter_context(nc.psum_tensor("c_psKV", [128, KVL], F32))
        NSEG = max(1, KVL // 512)
        SEGW = min(512, KVL)

        def f_sy(sy):
            ph_wait(sy)
            for j in range(NJOB):
                if j >= 1:
                    sy.wait_ge(sc_sc.h, 3 * j)
                sy.dma_start(out=c_xt[:, :],
                             in_=inp["xkv"][j * 128:(j + 1) * 128, :]).then_inc(sc_ld.h, 16)
                for dm in range(DC):
                    k = j * DC + dm
                    if k >= 2:
                        sy.wait_ge(sc_mm.h, k - 1)
                    sy.dma_start(out=c_wkv[k % 2][:, :],
                                 in_=dram["wkv_bf"][dm * 128:(dm + 1) * 128, :]
                                 ).then_inc(sc_w[k % 2].h, 16)
            sy.sem_inc(s_ph.h, 1)
        blk.sync(f_sy)

        def f_sc(sc):
            ph_wait(sc)
            sc.wait_ge(sc_gp.h, 1)
            for j in range(NJOB):
                sb, it = j // NKT, j % NKT
                sc.wait_ge(sc_ld.h, 16 * (j + 1))
                if j >= 1:
                    sc.wait_ge(sc_ktr.h, 2 * j - 1)
                emit_ln_scalar(sc, sc_sc, sc_gp, c_xt, D, c_xln,
                               gp_base=1 + 2 * j)
                sc.wait_ge(sc_ktr.h, 2 * j + 1)
                if j >= 1:
                    sc.wait_ge(sc_mm.h, j * DC)
                s3 = sc.copy(c_lnT[:].rearrange("p (c q) -> p c q", c=DC),
                             psT[:].rearrange("p (c q) -> p c q", c=DC))
                sc_ev.inc(s3)
                sc.wait_ge(sc_mm.h, (j + 1) * DC)
                vsrc = psKV[:].rearrange("p (g t h) -> p g t h", g=GL, t=2)
                vcr = VC[:].rearrange("p (s o) -> p s o", o=129)
                for gl in range(GL):
                    slot = it * G + 4 * gl + sb
                    sc.copy(vcr[:, slot:slot + 1, 0:128], vsrc[:, gl:gl + 1, 1, :])
                ksrc = psKV[:].rearrange("p (g t h) -> p g t h", g=GL, t=2)[:, :, 0, :]
                if j >= 1:
                    sc.wait_ge(sc_ktr.h, 2 * j)
                sc_ev.inc(sc.copy(c_kn[:].rearrange("p (g h) -> p g h", g=GL), ksrc))
                sc.wait_ge(sc_ktr.h, 2 * j + 2)
                kt_dst = KT[:].rearrange("p (a s t) -> p a s t", a=GL, s=4)[
                    :, :, sb, it * 128:(it + 1) * 128]
                sc_kev.inc(sc.copy(kt_dst,
                                   psT[:, 0:GL * 128].rearrange("p (g q) -> p g q", g=GL)))
            sc.sem_inc(s_ph.h, 1)
        blk.scalar(f_sc)

        def f_gp(gp):
            ph_wait(gp)
            gp.memset(VC[:, :], 1.0)
            gp.sem_inc(sc_gp.h, 1)
            for j in range(NJOB):
                emit_ln_gp(gp, sc_sc, sc_gp, sc_base=3 * j, w=D)
            gp.sem_inc(s_ph.h, 1)
        blk.gpsimd(f_gp)

        def f_te(te):
            ph_wait(te)
            for j in range(NJOB):
                te.wait_ge(sc_sc.h, 3 * j + 3)
                if j >= 1:
                    te.wait_ge(sc_kev.h, j)
                tr = None
                for c in range(DC):
                    tr = te.transpose(psT[:, c * 128:(c + 1) * 128],
                                      c_xln[:, c * 128:(c + 1) * 128], identB[:])
                sc_ktr.inc(tr)
                te.wait_ge(sc_ev.h, 2 * j)
                te.wait_ge(sc_ev.h, 2 * j + 1)
                for dm in range(DC):
                    k = j * DC + dm
                    te.wait_ge(sc_w[k % 2].h, 16 * (k // 2 + 1))
                    mm = None
                    for seg in range(NSEG):
                        mm = te.matmul(psKV[:, seg * SEGW:(seg + 1) * SEGW],
                                       c_lnT[:, dm * 128:(dm + 1) * 128],
                                       c_wkv[k % 2][:, seg * SEGW:(seg + 1) * SEGW],
                                       start=(dm == 0), stop=(dm == DC - 1))
                    sc_mm.inc(mm)
                te.wait_ge(sc_ev.h, 2 * j + 2)
                tr2 = None
                for g in range(GL):
                    tr2 = te.transpose(psT[:, g * 128:(g + 1) * 128],
                                       c_kn[:, g * 128:(g + 1) * 128], identB[:])
                sc_ktr.inc(tr2)
            te.sem_inc(s_ph.h, 1)
        blk.tensor(f_te)
        ces.close()
    ph_end(4)

    emit_tp("u", DQ_ST_U)

    with nc.Block() as blk:
        s2_q = Ctr("s2_q")
        s2_s = Ctr("s2_s")
        s2_e = Ctr("s2_e")
        s2_m = Ctr("s2_m")
        s2_c = Ctr("s2_c")
        s2_v = Ctr("s2_v")
        s2_t = Ctr("s2_t")
        s2_r = Ctr("s2_r")
        s2_o = Ctr("s2_o")
        aes = ExitStack()
        masks = aes.enter_context(nc.sbuf_tensor("masks", [128, 8 * 512], BF16))
        qTg = aes.enter_context(nc.sbuf_tensor("qTg", [128, DC * 512], BF16))
        pT = [aes.enter_context(nc.sbuf_tensor(f"pT{i}", [128, 512], BF16))
              for i in range(2)]
        pM = [aes.enter_context(nc.sbuf_tensor(f"pM{i}", [128, 512], BF16))
              for i in range(2)]
        cnS = [aes.enter_context(nc.sbuf_tensor(f"cnS{i}", [128, 128], BF16))
               for i in range(4)]
        crow = aes.enter_context(nc.sbuf_tensor("crow", [128, 512], BF16))
        linv = aes.enter_context(nc.sbuf_tensor("linv", [128, 1], F32))
        sps = [aes.enter_context(nc.psum_tensor(f"sps{i}", [128, 512], F32))
               for i in range(2)]
        pcs = [aes.enter_context(nc.psum_tensor(f"pcs{i}", [128, 512], F32))
               for i in range(4)]
        ctp = aes.enter_context(nc.psum_tensor("ctp", [128, 1024], BF16))

        sched = []
        for qg in range(NG):
            kvt_n = 8 * (qg + 1)
            for g in range(G):
                for hp in range(4):
                    sched.append((qg, g, g * 4 + hp, kvt_n))

        import bisect
        masked_units = []
        _u = 0
        for (qg, g, h, kvt_n) in sched:
            for kt in range(kvt_n):
                if kt >= kvt_n - 8:
                    masked_units.append(_u)
                _u += 1

        def mask_cnt_through(uu):
            return bisect.bisect_right(masked_units, uu)

        def f_sy(sy):
            ph_wait(sy)
            sy.dma_start(out=masks[:].rearrange("p (w q) -> p w q", w=8),
                         in_=inp["maskin"].rearrange("(w p) q -> p w q", p=128)
                         ).then_inc(s2_q.h, 16)
            for qg in range(NG):
                if qg >= 1:
                    prev_units = sum(s[3] for s in sched if s[0] < qg)
                    sy.wait_ge(s2_s.h, prev_units)
                for c in range(DC):
                    sy.dma_start(out=qTg[:, c * 512:(c + 1) * 512],
                                 in_=dram["qT"][c * 128:(c + 1) * 128,
                                                qg * 512:(qg + 1) * 512]
                                 ).then_inc(s2_q.h, 16)
            sy.sem_inc(s_ph.h, 1)
        blk.sync(f_sy)

        def f_te(te):
            ph_wait(te)
            u = 0
            grp = 0
            for (qg, g, h, kvt_n) in sched:
                te.wait_ge(s2_q.h, 16 + 16 * DC * (qg + 1))
                if grp >= 1:
                    te.wait_ge(s2_v.h, 4 * grp)
                for kt in range(kvt_n):
                    if u >= 2:
                        te.wait_ge(s2_e.h, u - 1)
                    s2_s.inc(te.matmul(sps[u % 2][:, :],
                                       KT[:, g * T + kt * 128: g * T + (kt + 1) * 128],
                                       qTg[:, h * 512:(h + 1) * 512],
                                       start=True, stop=True))
                    masked = (kt >= kvt_n - 8)
                    if masked:
                        te.wait_ge(s2_m.h, mask_cnt_through(u))
                    else:
                        te.wait_ge(s2_e.h, u + 1)
                    src = pM[kt & 1] if masked else pT[kt & 1]
                    mm = None
                    for qc in range(4):
                        mm = te.matmul(pcs[qc][:, 0:129], src[:, qc * 128:(qc + 1) * 128],
                                       VC[:, (kt * G + g) * 129:(kt * G + g) * 129 + 129],
                                       start=(kt == 0), stop=(kt == kvt_n - 1))
                    s2_c.inc(mm)
                    u += 1
                grp += 1
                for a in range(4):
                    te.wait_ge(s2_v.h, 4 * (grp - 1) + a + 1)
                    te.wait_ge(s2_r.h, 4 * (grp - 1) + a)
                    s2_t.inc(te.transpose(ctp[:, 0:128], cnS[a][:, :], identB[:]))
            te.sem_inc(s_ph.h, 1)
        blk.tensor(f_te)

        def f_sc(sc):
            ph_wait(sc)
            u = 0
            grp = 0
            for (qg, g, h, kvt_n) in sched:
                for kt in range(kvt_n):
                    sc.wait_ge(s2_s.h, u + 1)
                    if u >= 2:
                        sc.wait_ge(s2_c.h, u - 1)
                    s2_e.inc(sc.activation(pT[kt & 1][:, :],
                                           sps[u % 2][:, :], AF.Exp,
                                           bias=c_nb[:, :], scale=ATT_SCALE))
                    u += 1
                grp += 1
                sc.wait_ge(s2_c.h, sum(s[3] for s in sched[:grp]))
                if grp >= 2:
                    sc.wait_ge(s2_o.h, 16 * (grp - 1))
                for a in range(4):
                    if grp >= 2:
                        sc.wait_ge(s2_t.h, 4 * (grp - 2) + a + 1)
                    lap = pcs[a][:, 128:129]
                    sc.activation(linv[:, :], lap, AF.Ln)
                    sc.activation(linv[:, :], linv[:, :], AF.Exp, scale=-1.0)
                    s2_v.inc(sc.activation(cnS[a][:, :], pcs[a][:, 0:128],
                                           AF.Identity, scale=linv[:, :]))
                for a in range(4):
                    sc.wait_ge(s2_t.h, 4 * (grp - 1) + a + 1)
                    s2_r.inc(sc.copy(crow[:, a * 128:(a + 1) * 128], ctp[:, 0:128]))
            sc.sem_inc(s_ph.h, 1)
        blk.scalar(f_sc)

        def f_gp(gp):
            ph_wait(gp)
            u = 0
            grp = 0
            for (qg, g, h, kvt_n) in sched:
                for kt in range(kvt_n):
                    if kt >= kvt_n - 8:
                        w = kt - (kvt_n - 8)
                        gp.wait_ge(s2_e.h, u + 1)
                        if u >= 2:
                            gp.wait_ge(s2_c.h, u - 1)
                        s2_m.inc(gp.tensor_mul(pM[kt & 1][:, :], pT[kt & 1][:, :],
                                               masks[:, w * 512:(w + 1) * 512]))
                    u += 1
                grp += 1
                gp.wait_ge(s2_r.h, 4 * grp)
                gp.dma_start(out=dram["ctxT"][h * 128:(h + 1) * 128,
                                              qg * 512:(qg + 1) * 512],
                             in_=crow[:, :]).then_inc(s2_o.h, 16)
            gp.wait_ge(s2_o.h, s2_o.v)
            gp.sem_inc(s_ph.h, 1)
        blk.gpsimd(f_gp)
        aes.close()
    ph_end(4)
    kv_es.close()

    emit_tp("d", DQ_ST_D)

    p3_es = ExitStack()
    ctxS = p3_es.enter_context(nc.sbuf_tensor("ctxS", [128, DC * OWN], BF16))
    with nc.Block() as blk:
        s3_c = Ctr("s3_c")
        s3_w = [Ctr("s3_w0"), Ctr("s3_w1")]
        s3_mm = Ctr("s3_mm")
        s3_e1 = Ctr("s3_e1")
        s3_tr = Ctr("s3_tr")
        s3_e2 = Ctr("s3_e2")
        s3_st = Ctr("s3_st")
        pes = ExitStack()
        wop = [pes.enter_context(nc.sbuf_tensor(f"wop{i}", [128, DC * 128], BF16))
               for i in range(2)]
        oT = [pes.enter_context(nc.sbuf_tensor(f"oT{i}", [128, OWN], F32))
              for i in range(2)]
        oN = [pes.enter_context(nc.sbuf_tensor(f"oN{i}", [128, 128], F32))
              for i in range(2)]
        psO = [pes.enter_context(nc.psum_tensor(f"psO{i}", [128, 512], F32))
               for i in range(NG)]
        psN = [pes.enter_context(nc.psum_tensor(f"psN{i}", [128, 512], F32))
               for i in range(2)]

        def f_sy(sy):
            ph_wait(sy)
            for c in range(DC):
                sy.dma_start(out=ctxS[:, c * OWN:(c + 1) * OWN],
                             in_=dram["ctxT"][c * 128:(c + 1) * 128, :]).then_inc(s3_c.h, 16)
            for dm in range(DC):
                if dm >= 2:
                    sy.wait_ge(s3_mm.h, dm - 1)
                sy.dma_start(out=wop[dm % 2][:, :],
                             in_=dram["wo_bf"][dm * 128:(dm + 1) * 128, :]
                             ).then_inc(s3_w[dm % 2].h, 16)
            sy.sem_inc(s_ph.h, 1)
        blk.sync(f_sy)

        def f_te(te):
            ph_wait(te)
            te.wait_ge(s3_c.h, 16 * DC)
            ntr = 0
            for dm in range(DC):
                te.wait_ge(s3_w[dm % 2].h, 16 * (dm // 2 + 1))
                if dm >= 1:
                    te.wait_ge(s3_e1.h, NG * dm)
                mm = None
                for cd in range(DC):
                    for g in range(NG):
                        mm = te.matmul(psO[g][:, :],
                                       wop[dm % 2][:, cd * 128:(cd + 1) * 128],
                                       ctxS[:, cd * OWN + g * 512: cd * OWN + (g + 1) * 512],
                                       start=(cd == 0), stop=(cd == DC - 1))
                s3_mm.inc(mm)
                for i in range(NQT):
                    te.wait_ge(s3_e1.h, NG * dm + i // 4 + 1)
                    if ntr >= 1:
                        te.wait_ge(s3_e2.h, ntr)
                    s3_tr.inc(te.transpose(psN[ntr % 2][:, 0:128],
                                           oT[dm % 2][:, i * 128:(i + 1) * 128], identF[:]))
                    ntr += 1
            te.sem_inc(s_ph.h, 1)
        blk.tensor(f_te)

        def f_sc(sc):
            ph_wait(sc)
            ntr = 0
            for dm in range(DC):
                sc.wait_ge(s3_mm.h, dm + 1)
                if dm >= 2:
                    sc.wait_ge(s3_tr.h, (dm - 1) * NQT)
                for g in range(NG):
                    s3_e1.inc(sc.activation(oT[dm % 2][:, g * 512:(g + 1) * 512],
                                            psO[g][:, :], AF.Identity,
                                            bias=boS[:, dm:dm + 1]))
                for i in range(NQT):
                    sc.wait_ge(s3_tr.h, ntr + 1)
                    if ntr >= 1:
                        sc.wait_ge(s3_st.h, 16 * ntr)
                    s3_e2.inc(sc.copy(oN[ntr % 2][:, :], psN[ntr % 2][:, 0:128]))
                    ntr += 1
            sc.sem_inc(s_ph.h, 1)
        blk.scalar(f_sc)

        def f_gp(gp):
            ph_wait(gp)
            ntr = 0
            for dm in range(DC):
                for i in range(NQT):
                    gp.wait_ge(s3_e2.h, ntr + 1)
                    gp.dma_start(out=dram["aon"][i * 128:(i + 1) * 128,
                                                 dm * 128:(dm + 1) * 128],
                                 in_=oN[ntr % 2][:, :]).then_inc(s3_st.h, 16)
                    ntr += 1
            gp.wait_ge(s3_st.h, 16 * DC * NQT)
            gp.sem_inc(s_ph.h, 1)
        blk.gpsimd(f_gp)
        pes.close()
    ph_end(4)
    p3_es.close()

    with nc.Block() as blk:
        s4_l1 = [Ctr("s4_l1a"), Ctr("s4_l1b")]
        s4_l2 = [Ctr("s4_l2a"), Ctr("s4_l2b")]
        s4_ad = Ctr("s4_ad")
        s4_sc = Ctr("s4_sc")
        s4_gp = Ctr("s4_gp")
        s4_tr = Ctr("s4_tr")
        s4_ev = Ctr("s4_ev")
        s4_st = Ctr("s4_st")
        bes = ExitStack()
        b_ao = [bes.enter_context(nc.sbuf_tensor(f"b_ao{i}", [128, D], F32))
                for i in range(2)]
        b_xq = [bes.enter_context(nc.sbuf_tensor(f"b_xq{i}", [128, D], F32))
                for i in range(2)]
        b_x2 = [bes.enter_context(nc.sbuf_tensor(f"b_x2{i}", [128, D], F32))
                for i in range(2)]
        b_ln = [bes.enter_context(nc.sbuf_tensor(f"b_ln{i}", [128, D], BF16))
                for i in range(2)]
        b_lt = [bes.enter_context(nc.sbuf_tensor(f"b_lt{i}", [128, DC * 128], BF16))
                for i in range(2)]
        psL = [bes.enter_context(nc.psum_tensor(f"b_psL{i}", [128, DC * 128], BF16))
               for i in range(2)]

        def f_sy(sy):
            ph_wait(sy)
            for i in range(NQT):
                if i >= 2:
                    sy.wait_ge(s4_ad.h, i - 1)
                sy.dma_start(out=b_ao[i % 2][:, :],
                             in_=dram["aon"][i * 128:(i + 1) * 128, :]
                             ).then_inc(s4_l1[i % 2].h, 16)
                sy.dma_start(out=b_xq[i % 2][:, :],
                             in_=inp["xq"][i * 128:(i + 1) * 128, :]
                             ).then_inc(s4_l2[i % 2].h, 16)
            sy.sem_inc(s_ph.h, 1)
        blk.sync(f_sy)

        def f_gp(gp):
            ph_wait(gp)
            for i in range(NQT):
                gp.wait_ge(s4_l1[i % 2].h, 16 * (i // 2 + 1))
                gp.wait_ge(s4_l2[i % 2].h, 16 * (i // 2 + 1))
                if i >= 2:
                    gp.wait_ge(s4_st.h, 32 * (i - 1))
                    gp.wait_ge(s4_sc.h, 3 * (i - 2) + 3)
                s4_ad.inc(gp.tensor_add(b_x2[i % 2][:, :], b_ao[i % 2][:, :],
                                        b_xq[i % 2][:, :]))
                emit_ln_gp(gp, s4_sc, s4_gp, sc_base=3 * i, w=D)
                gp.wait_ge(s4_sc.h, 3 * i + 3)
                gp.dma_start(out=dram["x2f"][i * 128:(i + 1) * 128, :],
                             in_=b_x2[i % 2][:, :]).then_inc(s4_st.h, 16)
                gp.wait_ge(s4_ev.h, i + 1)
                gp.dma_start(out=dram["ln2T"].rearrange("(c p) t -> p c t", p=128)
                             [:, :, i * 128:(i + 1) * 128],
                             in_=b_lt[i % 2][:].rearrange("p (c q) -> p c q", c=DC)
                             ).then_inc(s4_st.h, 16)
            gp.wait_ge(s4_st.h, 32 * NQT)
            gp.sem_inc(s_ph.h, 1)
        blk.gpsimd(f_gp)

        def f_sc(sc):
            ph_wait(sc)
            for i in range(NQT):
                sc.wait_ge(s4_ad.h, i + 1)
                if i >= 2:
                    sc.wait_ge(s4_tr.h, i - 1)
                emit_ln_scalar(sc, s4_sc, s4_gp, b_x2[i % 2], D, b_ln[i % 2],
                               gp_base=2 * i)
                sc.wait_ge(s4_tr.h, i + 1)
                if i >= 2:
                    sc.wait_ge(s4_st.h, 32 * (i - 1))
                s4_ev.inc(sc.copy(b_lt[i % 2][:, :], psL[i % 2][:, :]))
            sc.sem_inc(s_ph.h, 1)
        blk.scalar(f_sc)

        def f_te(te):
            ph_wait(te)
            for i in range(NQT):
                te.wait_ge(s4_sc.h, 3 * i + 3)
                if i >= 2:
                    te.wait_ge(s4_ev.h, i - 1)
                tr = None
                for c in range(DC):
                    tr = te.transpose(psL[i % 2][:, c * 128:(c + 1) * 128],
                                      b_ln[i % 2][:, c * 128:(c + 1) * 128], identB[:])
                s4_tr.inc(tr)
            te.sem_inc(s_ph.h, 1)
        blk.tensor(f_te)
        bes.close()
    ph_end(4)
    dq_es.close()

    t3_es = ExitStack()
    t3S = t3_es.enter_context(nc.sbuf_tensor("t3S", [R, OWN], BF16))
    ffn_es = ExitStack()
    lnS = ffn_es.enter_context(nc.sbuf_tensor("lnS", [128, DC * OWN], BF16))
    glaS = ffn_es.enter_context(nc.sbuf_tensor("glaS", [128, DC * R], BF16))
    ulaS = ffn_es.enter_context(nc.sbuf_tensor("ulaS", [128, DC * R], BF16))
    dlaS = ffn_es.enter_context(nc.sbuf_tensor("dlaS", [128, FC * R], BF16))
    t1S = ffn_es.enter_context(nc.sbuf_tensor("t1S", [R, OWN], BF16))
    t2S = ffn_es.enter_context(nc.sbuf_tensor("t2S", [R, OWN], BF16))
    with nc.Block() as blk:
        sf_l = Ctr("sf_l")
        sf_mm = Ctr("sf_mm")
        sf_ev = Ctr("sf_ev")
        fes = ExitStack()
        pst = fes.enter_context(nc.psum_tensor("f0_pst", [128, 512], F32))

        def f_sy(sy):
            ph_wait(sy)
            sy.wait_ge(s_dq_st.h, DQ_ST_D)
            for c in range(DC):
                sy.dma_start(out=lnS[:, c * OWN:(c + 1) * OWN],
                             in_=dram["ln2T"][c * 128:(c + 1) * 128, :]).then_inc(sf_l.h, 16)
            sy.sem_inc(s_ph.h, 1)
        blk.sync(f_sy)

        def f_sc(sc):
            ph_wait(sc)
            tmp = fes.enter_context(nc.sbuf_tensor("la_tmp", [128, max(DC, FC) * R], F32))
            n = 0
            for nm, dst, nch in (("gla", glaS, DC), ("ula", ulaS, DC), ("dla", dlaS, FC)):
                sc.dma_start(out=tmp[:, :nch * R].rearrange("p (c r) -> p c r", r=R),
                             in_=inp[nm].rearrange("(c p) r -> p c r", p=128)
                             ).then_inc(sf_l.h, 16)
                n += 16
                sc.wait_ge(sf_l.h, 16 * DC + n)
                sc.copy(dst[:, :], tmp[:, :nch * R]).then_inc(sf_l.h, 1)
                n += 1
            k = 0
            for dst in (t1S, t2S):
                for g in range(NG):
                    sc.wait_ge(sf_mm.h, k + 1)
                    sf_ev.inc(sc.copy(dst[0:R, g * 512:(g + 1) * 512], pst[0:R, :]))
                    k += 1
            sc.sem_inc(s_ph.h, 1)
        blk.scalar(f_sc)

        def f_te(te):
            ph_wait(te)
            te.wait_ge(sf_l.h, 16 * DC + 17)
            k = 0
            for la in (glaS, ulaS):
                if la is ulaS:
                    te.wait_ge(sf_l.h, 16 * DC + 34)
                for g in range(NG):
                    if k >= 1:
                        te.wait_ge(sf_ev.h, k)
                    mm = None
                    for dm in range(DC):
                        mm = te.matmul(pst[0:R, :], la[:, dm * R:(dm + 1) * R],
                                       lnS[:, dm * OWN + g * 512: dm * OWN + (g + 1) * 512],
                                       start=(dm == 0), stop=(dm == DC - 1))
                    sf_mm.inc(mm)
                    k += 1
            te.sem_inc(s_ph.h, 1)
        blk.tensor(f_te)
        fes.close()
    ph_end(3)

    with nc.Block() as blk:
        sa_w = [Ctr("sa_w0"), Ctr("sa_w1")]
        sa_mm1 = Ctr("sa_mm1")
        sa_mm2 = Ctr("sa_mm2")
        sa_r = Ctr("sa_r")
        sa_x = Ctr("sa_x")
        sa_st = Ctr("sa_st")
        aes2 = ExitStack()
        gblk = [aes2.enter_context(nc.sbuf_tensor(f"fa_g{i}", [128, DC * 128], BF16))
                for i in range(2)]
        ublk = [aes2.enter_context(nc.sbuf_tensor(f"fa_u{i}", [128, DC * 128], BF16))
                for i in range(2)]
        lbg = [aes2.enter_context(nc.sbuf_tensor(f"fa_lbg{i}", [R, 128], BF16))
               for i in range(2)]
        lbu = [aes2.enter_context(nc.sbuf_tensor(f"fa_lbu{i}", [R, 128], BF16))
               for i in range(2)]
        r1 = aes2.enter_context(nc.sbuf_tensor("fa_r1", [128, OWN], BF16))
        x3b = [aes2.enter_context(nc.sbuf_tensor(f"fa_x3{i}", [128, OWN], BF16))
               for i in range(2)]
        psY1 = [aes2.enter_context(nc.psum_tensor(f"fa_ps1{i}", [128, 512], F32))
                for i in range(NG)]
        psY2 = [aes2.enter_context(nc.psum_tensor(f"fa_ps2{i}", [128, 512], F32))
                for i in range(NG)]

        def f_sy(sy):
            ph_wait(sy)
            sy.wait_ge(s_ag.h, 2)
            for fs in range(FC):
                if fs >= 2:
                    sy.wait_ge(sa_mm2.h, fs - 1)
                for quarter in range(4):
                    q0 = quarter * (D // 4)
                    sy.dma_start(
                        out=gblk[fs % 2][:, quarter * (DC // 4) * 128:
                                         (quarter + 1) * (DC // 4) * 128]
                        .rearrange("p (c f) -> p c f", f=128),
                        in_=dram["gath_g"][q0:q0 + D // 4, fs * 128:(fs + 1) * 128]
                        .rearrange("(c p) f -> p c f", p=128)).then_inc(sa_w[fs % 2].h, 16)
                    sy.dma_start(
                        out=ublk[fs % 2][:, quarter * (DC // 4) * 128:
                                         (quarter + 1) * (DC // 4) * 128]
                        .rearrange("p (c f) -> p c f", f=128),
                        in_=dram["gath_u"][q0:q0 + D // 4, fs * 128:(fs + 1) * 128]
                        .rearrange("(c p) f -> p c f", p=128)).then_inc(sa_w[fs % 2].h, 16)
                sy.dma_start(out=lbg[fs % 2][:, :],
                             in_=dram["glb_bf"][:, fs * 128:(fs + 1) * 128]
                             ).then_inc(sa_w[fs % 2].h, 16)
                sy.dma_start(out=lbu[fs % 2][:, :],
                             in_=dram["ulb_bf"][:, fs * 128:(fs + 1) * 128]
                             ).then_inc(sa_w[fs % 2].h, 16)
            sy.sem_inc(s_ph.h, 1)
        blk.sync(f_sy)

        def f_te(te):
            ph_wait(te)
            for fs in range(FC):
                te.wait_ge(sa_w[fs % 2].h, 16 * 10 * (fs // 2 + 1))
                if fs >= 1:
                    te.wait_ge(sa_r.h, fs)
                mm = None
                for dm in range(DC):
                    for g in range(NG):
                        mm = te.matmul(psY1[g][:, :],
                                       gblk[fs % 2][:, dm * 128:(dm + 1) * 128],
                                       lnS[:, dm * OWN + g * 512: dm * OWN + (g + 1) * 512],
                                       start=(dm == 0), stop=False)
                for g in range(NG):
                    mm = te.matmul(psY1[g][:, :], lbg[fs % 2][0:R, :],
                                   t1S[0:R, g * 512:(g + 1) * 512],
                                   start=False, stop=True)
                sa_mm1.inc(mm)
                if fs >= 1:
                    te.wait_ge(sa_x.h, fs)
                for dm in range(DC):
                    for g in range(NG):
                        mm = te.matmul(psY2[g][:, :],
                                       ublk[fs % 2][:, dm * 128:(dm + 1) * 128],
                                       lnS[:, dm * OWN + g * 512: dm * OWN + (g + 1) * 512],
                                       start=(dm == 0), stop=False)
                for g in range(NG):
                    mm = te.matmul(psY2[g][:, :], lbu[fs % 2][0:R, :],
                                   t2S[0:R, g * 512:(g + 1) * 512],
                                   start=False, stop=True)
                sa_mm2.inc(mm)
            te.sem_inc(s_ph.h, 1)
        blk.tensor(f_te)

        def f_sc(sc):
            ph_wait(sc)
            for fs in range(FC):
                sc.wait_ge(sa_mm1.h, fs + 1)
                if fs >= 1:
                    sc.wait_ge(sa_x.h, fs)
                last = None
                for g in range(NG):
                    last = sc.activation(r1[:, g * 512:(g + 1) * 512],
                                         psY1[g][:, :], AF.Relu)
                sa_r.inc(last)
            sc.sem_inc(s_ph.h, 1)
        blk.scalar(f_sc)

        def f_ve(ve):
            ph_wait(ve)
            for fs in range(FC):
                ve.wait_ge(sa_r.h, fs + 1)
                ve.wait_ge(sa_mm2.h, fs + 1)
                if fs >= 2:
                    ve.wait_ge(sa_st.h, 16 * (fs - 1))
                last = None
                for g in range(NG):
                    last = ve.tensor_mul(x3b[fs % 2][:, g * 512:(g + 1) * 512],
                                         r1[:, g * 512:(g + 1) * 512],
                                         psY2[g][:, :])
                sa_x.inc(last)
            ve.sem_inc(s_ph.h, 1)
        blk.vector(f_ve)

        def f_gp(gp):
            ph_wait(gp)
            for fs in range(FC):
                gp.wait_ge(sa_x.h, fs + 1)
                gp.dma_start(out=dram["x3T"][fs * 128:(fs + 1) * 128, :],
                             in_=x3b[fs % 2][:, :]).then_inc(sa_st.h, 16)
            gp.wait_ge(sa_st.h, 16 * FC)
            gp.sem_inc(s_ph.h, 1)
        blk.gpsimd(f_gp)
        aes2.close()
    ph_end(5)

    with nc.Block() as blk:
        sb_l = [Ctr("sb_l0"), Ctr("sb_l1")]
        sb_mm = Ctr("sb_mm")
        fes2 = ExitStack()
        xpan = [fes2.enter_context(nc.sbuf_tensor(f"a2_x{i}", [128, OWN], BF16))
                for i in range(2)]
        pst3 = [fes2.enter_context(nc.psum_tensor(f"a2_ps{i}", [128, 512], F32))
                for i in range(NG)]

        def f_sy(sy):
            ph_wait(sy)
            for fs in range(FC):
                if fs >= 2:
                    sy.wait_ge(sb_mm.h, fs - 1)
                sy.dma_start(out=xpan[fs % 2][:, :],
                             in_=dram["x3T"][fs * 128:(fs + 1) * 128, :]
                             ).then_inc(sb_l[fs % 2].h, 16)
            sy.sem_inc(s_ph.h, 1)
        blk.sync(f_sy)

        def f_te(te):
            ph_wait(te)
            for fs in range(FC):
                te.wait_ge(sb_l[fs % 2].h, 16 * (fs // 2 + 1))
                mm = None
                for g in range(NG):
                    mm = te.matmul(pst3[g][0:R, :], dlaS[:, fs * R:(fs + 1) * R],
                                   xpan[fs % 2][:, g * 512:(g + 1) * 512],
                                   start=(fs == 0), stop=(fs == FC - 1))
                sb_mm.inc(mm)
            te.sem_inc(s_ph.h, 1)
        blk.tensor(f_te)

        def f_sc(sc):
            ph_wait(sc)
            sc.wait_ge(sb_mm.h, FC)
            last = None
            for g in range(NG):
                last = sc.copy(t3S[0:R, g * 512:(g + 1) * 512], pst3[g][0:R, :])
            last.then_inc(s_ph.h, 1)
        blk.scalar(f_sc)
        fes2.close()
    ph_end(3)
    ffn_es.close()

    with nc.Block() as blk:
        sc_wd = Ctr("sc_wd")
        sc_x3 = [Ctr("sc_x30"), Ctr("sc_x31")]
        sc_x2 = Ctr("sc_x2")
        sc_mmb = Ctr("sc_mmb")
        sc_ad = Ctr("sc_ad")
        sc_st = Ctr("sc_st")
        wes = ExitStack()
        dlbS = wes.enter_context(nc.sbuf_tensor("dlbS", [R, D], BF16))
        wdq = wes.enter_context(nc.sbuf_tensor("wdq", [128, 1024 * FC], BF16))
        x3s = [wes.enter_context(nc.sbuf_tensor(f"fb_x3{i}", [128, 512], BF16))
               for i in range(2)]
        x2s = wes.enter_context(nc.sbuf_tensor("fb_x2", [128, 1024], F32))
        outb = wes.enter_context(nc.sbuf_tensor("fb_o", [128, 1024], F32))
        psB2 = [wes.enter_context(nc.psum_tensor(f"fb_ps{i}", [128, 1024], F32))
                for i in range(4)]

        NDQ = D // 1024
        NBQ = 1024 // SH
        PMQ = FC + 1

        def mm_after(dq, quad, fc=None):
            base = (dq * NG + quad) * PMQ
            return base + (PMQ if fc is None else fc + 1)

        x3_trip = []
        for dq in range(NDQ):
            for quad in range(NG):
                for fc in range(FC):
                    x3_trip.append((dq, quad, fc))

        def f_sy(sy):
            ph_wait(sy)
            sy.wait_ge(s_ag.h, 3)
            nx = 0
            for dq in range(NDQ):
                if dq >= 1:
                    sy.wait_ge(sc_mmb.h, dq * NG * PMQ)
                for b2 in range(NBQ):
                    blkr = NBQ * dq + b2
                    for fc in range(FC):
                        sy.dma_start(
                            out=wdq[:, (b2 * FC + fc) * SH:(b2 * FC + fc + 1) * SH],
                            in_=dram["gath_d"][blkr * F + fc * 128: blkr * F + (fc + 1) * 128, :]
                            ).then_inc(sc_wd.h, 16)
                for quad in range(NG):
                    for fc in range(FC):
                        if nx >= 2:
                            d2, q2, f2 = x3_trip[nx - 2]
                            sy.wait_ge(sc_mmb.h, mm_after(d2, q2, f2))
                        sy.dma_start(out=x3s[nx % 2][:, :],
                                     in_=dram["x3T"][fc * 128:(fc + 1) * 128,
                                                     quad * 512:(quad + 1) * 512]
                                     ).then_inc(sc_x3[nx % 2].h, 16)
                        nx += 1
                    for tt in range(4):
                        k = (dq * NG + quad) * 4 + tt
                        if k >= 1:
                            sy.wait_ge(sc_ad.h, k)
                        row = quad * 512 + tt * 128
                        sy.dma_start(out=x2s[:, :],
                                     in_=dram["x2f"][row:row + 128,
                                                     dq * 1024:(dq + 1) * 1024]
                                     ).then_inc(sc_x2.h, 16)
            sy.sem_inc(s_ph.h, 1)
        blk.sync(f_sy)

        s_dlb = Ctr("s_dlb")

        def f_sc(sc):
            ph_wait(sc)
            sc.dma_start(out=dlbS[:, :], in_=dram["dlb_bf"][:, :]).then_inc(s_dlb.h, 16)
            sc.sem_inc(s_ph.h, 1)
        blk.scalar(f_sc)

        def f_te(te):
            ph_wait(te)
            te.wait_ge(s_dlb.h, 16)
            nx = 0
            for dq in range(NDQ):
                te.wait_ge(sc_wd.h, 16 * NBQ * FC * (dq + 1))
                for quad in range(NG):
                    base = dq * NG + quad
                    if base >= 1:
                        te.wait_ge(sc_ad.h, base * 4)
                    BANKF = max(1, 512 // SH)
                    for fc in range(FC):
                        te.wait_ge(sc_x3[nx % 2].h, 16 * (nx // 2 + 1))
                        mm = None
                        for tt in range(4):
                            for mv in range(NBQ):
                                mm = te.matmul(
                                    psB2[tt][:, mv * SH:(mv + 1) * SH],
                                    x3s[nx % 2][:, tt * 128:(tt + 1) * 128],
                                    wdq[:, (mv * FC + fc) * SH:(mv * FC + fc + 1) * SH],
                                    start=(fc == 0 and mv % BANKF == 0), stop=False)
                        sc_mmb.inc(mm)
                        nx += 1
                    mm = None
                    for tt in range(4):
                        row = quad * 512 + tt * 128
                        for mv in range(NBQ):
                            mm = te.matmul(
                                psB2[tt][:, mv * SH:(mv + 1) * SH],
                                t3S[0:R, row:row + 128],
                                dlbS[0:R, dq * 1024 + mv * SH: dq * 1024 + (mv + 1) * SH],
                                start=False, stop=(mv % BANKF == BANKF - 1))
                    sc_mmb.inc(mm)
            te.sem_inc(s_ph.h, 1)
        blk.tensor(f_te)

        def f_ve(ve):
            ph_wait(ve)
            for dq in range(NDQ):
                for quad in range(NG):
                    base = (dq * NG + quad)
                    for tt in range(4):
                        k = base * 4 + tt
                        ve.wait_ge(sc_mmb.h, mm_after(dq, quad))
                        ve.wait_ge(sc_x2.h, 16 * (k + 1))
                        if k >= 1:
                            ve.wait_ge(sc_st.h, 16 * k)
                        s2 = ve.tensor_add(outb[:, :], psB2[tt][:, :], x2s[:, :])
                        sc_ad.inc(s2)
            ve.sem_inc(s_ph.h, 1)
        blk.vector(f_ve)

        def f_gp(gp):
            ph_wait(gp)
            k = 0
            for dq in range(NDQ):
                for quad in range(NG):
                    for tt in range(4):
                        gp.wait_ge(sc_ad.h, k + 1)
                        gp.dma_start(
                            out=out[quad * 512 + tt * 128: quad * 512 + tt * 128 + 128,
                                    dq * 1024:(dq + 1) * 1024],
                            in_=outb[:, :]).then_inc(sc_st.h, 16)
                        k += 1
            gp.wait_ge(sc_st.h, 16 * NDQ * NG * 4)
            gp.sem_inc(s_ph.h, 1)
        blk.gpsimd(f_gp)
        wes.close()
    ph_end(5)
    t3_es.close()

    if debug_names:
        with nc.Block() as blk:
            sd = Ctr("s_dbg")
            dbs = ExitStack()
            dbF = dbs.enter_context(nc.sbuf_tensor("dbgF", [128, 16384], F32))
            dbB = dbs.enter_context(nc.sbuf_tensor("dbgB", [128, 16384], BF16))

            def f_gp(gp):
                ph_wait(gp)
                for nm in debug_names:
                    src = dram[nm]
                    dst = dbg_out[nm]
                    rows, cols = src.shape
                    buf = dbB if nm not in ("aon", "x2f") else dbF
                    maxc = 16384
                    for r0 in range(0, rows, 128):
                        rr = min(128, rows - r0)
                        for c0 in range(0, cols, maxc):
                            cc = min(maxc, cols - c0)
                            gp.dma_start(out=buf[0:rr, 0:cc],
                                         in_=src[r0:r0 + rr, c0:c0 + cc]).then_inc(sd.h, 16)
                            gp.wait_ge(sd.h, sd.v + 16)
                            sd.v += 16
                            gp.dma_start(out=dst[r0:r0 + rr, c0:c0 + cc],
                                         in_=buf[0:rr, 0:cc]).then_inc(sd.h, 16)
                            gp.wait_ge(sd.h, sd.v + 16)
                            sd.v += 16
                gp.sem_inc(s_ph.h, 1)
            blk.gpsimd(f_gp)
            dbs.close()
        ph_end(1)

    es.close()
    return nc



def _prep_core_inputs(cfg, core, x, wq, wkv, wo, bo, gcod, gam, ucod, uam,
                      dcod, dam, gla, glb, ula, ulb, dla, dlb):
    T, D, F = cfg["T"], cfg["D"], cfg["F"]
    DC = D // 128
    OWN = T // 2
    SH = D // 8
    bi, h = core // 2, core % 2
    own_rows = np.concatenate([np.arange((2 * j + h) * 128, (2 * j + h) * 128 + 128)
                               for j in range(OWN // 128)])
    m = np.zeros((8, 128, 4, 128), dtype=np.float32)
    for w in range(8):
        for s in range(4):
            d = 2 * s + h
            if w < d:
                m[w, :, s, :] = 1.0
            elif w == d:
                m[w, :, s, :] = np.triu(np.ones((128, 128), np.float32))
    mask = m.reshape(1024, 512).astype(ml_dtypes.bfloat16)

    wqr = np.ascontiguousarray(
        wq.reshape(DC, 128, DC, 128).transpose(2, 1, 0, 3).reshape(D, D))
    wor = np.ascontiguousarray(
        wo.reshape(DC, 128, DC, 128).transpose(2, 1, 0, 3).reshape(D, D))

    return {
        "xkv": np.ascontiguousarray(x.reshape(-1, D)),
        "xq": np.ascontiguousarray(x[bi][own_rows]),
        "wqr": wqr,
        "wkv": np.ascontiguousarray(wkv[:, bi * 512:(bi + 1) * 512]),
        "wor": wor,
        "bo": np.ascontiguousarray(bo.reshape(D, 1)),
        "maskin": mask,
        "gcod": np.ascontiguousarray(gcod[:, core * SH:(core + 1) * SH]),
        "gam": np.ascontiguousarray(gam[:, core * (SH // 64):(core + 1) * (SH // 64)]),
        "ucod": np.ascontiguousarray(ucod[:, core * SH:(core + 1) * SH]),
        "uam": np.ascontiguousarray(uam[:, core * (SH // 64):(core + 1) * (SH // 64)]),
        "dcod": np.ascontiguousarray(dcod[core * SH:(core + 1) * SH, :]),
        "dam": np.ascontiguousarray(dam[core * SH:(core + 1) * SH, :]),
        "gla": gla, "glb": glb, "ula": ula, "ulb": ulb, "dla": dla, "dlb": dlb,
    }, own_rows


def kernel(L0_attn_x, L0_attn_Wq, L0_attn_Wkv, L0_attn_Wo, L0_attn_bo,
           L0_ffn_g_codes, L0_ffn_g_absmax, L0_ffn_g_lora_a, L0_ffn_g_lora_b,
           L0_ffn_u_codes, L0_ffn_u_absmax, L0_ffn_u_lora_a, L0_ffn_u_lora_b,
           L0_ffn_d_codes, L0_ffn_d_absmax, L0_ffn_d_lora_a, L0_ffn_d_lora_b):
    cfg = full_cfg()
    T, D, F = cfg["T"], cfg["D"], cfg["F"]
    x = np.asarray(L0_attn_x, dtype=np.float32)
    wq = np.asarray(L0_attn_Wq, np.float32)
    wkv = np.asarray(L0_attn_Wkv, np.float32)
    wo = np.asarray(L0_attn_Wo, np.float32)
    bo = np.asarray(L0_attn_bo, np.float32)
    gcod = np.asarray(L0_ffn_g_codes, np.int32)
    gam = np.asarray(L0_ffn_g_absmax, np.float32)
    ucod = np.asarray(L0_ffn_u_codes, np.int32)
    uam = np.asarray(L0_ffn_u_absmax, np.float32)
    dcod = np.asarray(L0_ffn_d_codes, np.int32)
    dam = np.asarray(L0_ffn_d_absmax, np.float32)
    gla = np.asarray(L0_ffn_g_lora_a, np.float32)
    glb = np.asarray(L0_ffn_g_lora_b, np.float32)
    ula = np.asarray(L0_ffn_u_lora_a, np.float32)
    ulb = np.asarray(L0_ffn_u_lora_b, np.float32)
    dla = np.asarray(L0_ffn_d_lora_a, np.float32)
    dlb = np.asarray(L0_ffn_d_lora_b, np.float32)

    nc = build_nc(cfg)
    in_maps = []
    own_rows_all = []
    for core in range(8):
        im, own_rows = _prep_core_inputs(cfg, core, x, wq, wkv, wo, bo,
                                         gcod, gam, ucod, uam, dcod, dam,
                                         gla, glb, ula, ulb, dla, dlb)
        in_maps.append(im)
        own_rows_all.append(own_rows)

    res = run_bass_kernel_spmd(nc, in_maps, core_ids=list(range(8)))

    B = x.shape[0]
    outp = np.empty((B, T, D), dtype=np.float32)
    for core in range(8):
        bi = core // 2
        outp[bi][own_rows_all[core]] = res.results[core]["out"]
    return outp
